# version 01 (final)
# baseline (speedup 1.0000x reference)
import contextlib
import numpy as np
import ml_dtypes
import concourse.bass as bass
import concourse.mybir as mybir
from concourse.bass_utils import run_bass_kernel_spmd

"""Shared infrastructure for the mk_prb_d2 megakernel: task-graph emitter on
raw Bass (explicit semaphores, computed automatically from declared deps) and
host-side layout helpers."""

F32 = mybir.dt.float32
BF16 = mybir.dt.bfloat16
AF = mybir.ActivationFunctionType
ALU = mybir.AluOpType

ENGINES = ("sync", "tensor", "scalar", "vector", "gpsimd")


class Task:
    __slots__ = ("engine", "idx", "val", "label", "ch", "is_dma", "key")

    def __init__(self, engine, idx, label, ch, is_dma):
        self.engine = engine
        self.idx = idx
        self.val = None
        self.label = label
        self.ch = ch
        self.is_dma = is_dma
        self.key = (engine, ch)


class Prog:

    def __init__(self, nc):
        self.nc = nc
        self.tasks = {e: [] for e in ENGINES}

    def add(self, engine, fn, deps=(), step=None, label="", ch=0):
        is_dma = step == 16
        if step is None:
            step = 1
        t = Task(engine, len(self.tasks[engine]), label, ch if is_dma else 0, is_dma)
        deps = [d for d in deps if d is not None]
        self.tasks[engine].append((t, fn, deps, step))
        return t

    def dma(self, engine, out, in_, deps=(), label="", ch=0, **kw):
        def fn(eng):
            return eng.dma_start(out=out, in_=in_, **kw)
        return self.add(engine, fn, deps, step=16, label=label, ch=ch)

    def build(self, ctx):
        nc = self.nc
        counters = {}
        for e in ENGINES:
            for (t, fn, deps, step) in self.tasks[e]:
                key = t.key if t.is_dma else (e, "c")
                t.key = key
                counters[key] = counters.get(key, 0) + step
                t.val = counters[key]
        sems = {}
        for key in counters:
            sems[key] = ctx.enter_context(
                nc.semaphore(f"s_{key[0]}_{key[1]}"))
        block = ctx.enter_context(nc.Block())
        for e in ENGINES:
            if not self.tasks[e]:
                continue
            tasks = self.tasks[e]

            def emit(eng, e=e, tasks=tasks):
                waited = {}
                prev_on_ch = {}
                for (t, fn, deps, step) in tasks:
                    wants = []
                    if t.is_dma:
                        pv = prev_on_ch.get(t.key)
                        if pv is not None:
                            wants.append((t.key, pv))
                    for d in deps:
                        if d.is_dma:
                            if d.key == t.key and t.is_dma:
                                continue
                            wants.append((d.key, d.val))
                        else:
                            if d.engine == e:
                                continue
                            wants.append((d.key, d.val))
                    for key, val in wants:
                        if waited.get(key, -1) >= val:
                            continue
                        eng.wait_ge(sems[key], val)
                        waited[key] = val
                    last = fn(eng)
                    assert last is not None, f"task {t.label} returned no instruction"
                    last.then_inc(sems[t.key], step)
                    if t.is_dma:
                        prev_on_ch[t.key] = t.val

            getattr(block, e)(emit)
        return sems



def bf16(x):
    return np.asarray(x).astype(ml_dtypes.bfloat16)


def f32c(x):
    return np.ascontiguousarray(x, dtype=np.float32)


def core_token_rows(core):
    h = core & 1
    return [2 * j + h for j in range(16)]


def core_slice_tokens(X, core):
    b = core >> 1
    tiles = core_token_rows(core)
    return np.concatenate([X[b, t * 128:(t + 1) * 128] for t in tiles], axis=0)


def core_positions(core):
    tiles = core_token_rows(core)
    return np.concatenate([np.arange(t * 128, (t + 1) * 128) for t in tiles])


def kperm_local():
    perm = []
    for a in range(4):
        base = a * 256
        perm.extend(range(base, base + 256, 2))
        perm.extend(range(base + 1, base + 256, 2))
    perm.extend(range(1024, 2048))
    return np.array(perm)


def rot_tables(positions):
    inv_freq = (1.0 / (10000.0 ** (np.arange(0, 1024, 2, dtype=np.float32) / 1024.0))).astype(np.float32)
    pos = positions.astype(np.float32)
    ang = pos[None, :] * inv_freq[:, None]
    return np.cos(ang).astype(np.float32), np.sin(ang).astype(np.float32)


def wht_perm_sigma1():
    idx = np.zeros((16, 128), dtype=np.int64)
    for m in range(16):
        for pp in range(128):
            q = pp % 16
            r = pp // 16
            idx[m, pp] = q * 128 + m * 8 + r
    return idx.reshape(-1)


def wht_perm_sigma2():
    idx = np.zeros((64, 128), dtype=np.int64)
    for m in range(64):
        for pp in range(128):
            j = pp % 64
            p0 = pp // 64
            idx[m, pp] = j * 128 + 2 * m + p0
    return idx.reshape(-1)


def hadamard(n):
    H = np.array([[1.0]], dtype=np.float32)
    while H.shape[0] < n:
        H = np.block([[H, H], [H, -H]])
    return H.astype(np.float32)


def block_diag_h(nblk, hsz):
    H = hadamard(hsz)
    out = np.zeros((nblk * hsz, nblk * hsz), dtype=np.float32)
    for i in range(nblk):
        out[i * hsz:(i + 1) * hsz, i * hsz:(i + 1) * hsz] = H
    return out

"""Launch 1 (P1): outer LN -> s01/xn0bf spills; inner LN -> Xn; QKV projection
(streamed W_in, permuted layout); rotary on K,V; cos-sim partial A.
Per core: 2048 tokens (interleaved 128-tiles), feature-major layouts."""

P = 128
NT = 2048
TT = 512
NTILE = NT // TT
D = 2048
DC = 16
M3 = 48
EPS = 1e-5
AX = mybir.AxisListType.X


def build_p1(nc, ctx):
    x0 = nc.dram_tensor("x0", [D, NT], F32, kind="ExternalInput").ap()
    winp = nc.dram_tensor("winp", [D, 3 * D], F32, kind="ExternalInput").ap()
    bint_d = nc.dram_tensor("binp", [P, M3], F32, kind="ExternalInput").ap()
    cosT = nc.dram_tensor("cosT", [512, NT], F32, kind="ExternalInput").ap()
    sinT = nc.dram_tensor("sinT", [512, NT], F32, kind="ExternalInput").ap()
    gbin = nc.dram_tensor("gbin", [P, 2 * DC], F32, kind="ExternalInput").ap()
    s01 = nc.dram_tensor("s01", [D, NT], F32, kind="ExternalOutput").ap()
    xn0bf = nc.dram_tensor("xn0bf", [D, NT], BF16, kind="ExternalOutput").ap()
    qbf = nc.dram_tensor("qbf", [D, NT], BF16, kind="ExternalOutput").ap()
    apart = nc.dram_tensor("apart", [P, 2 * DC], F32, kind="ExternalOutput").ap()
    kvsp = nc.dram_tensor("kvsp", [2, DC, P, NT], BF16).ap()

    scrF = ctx.enter_context(nc.sbuf_tensor("scrF", [P, 18432], F32))
    scrB = ctx.enter_context(nc.sbuf_tensor("scrB", [P, 49152], BF16))
    rows = ctx.enter_context(nc.sbuf_tensor("rows", [1, 6144], F32))
    onesb = ctx.enter_context(nc.sbuf_tensor("onesb", [P, 1], BF16))
    onesf = ctx.enter_context(nc.sbuf_tensor("onesf", [1, P], F32))
    gbt = ctx.enter_context(nc.sbuf_tensor("gbt", [P, 2 * DC], F32))
    bint = ctx.enter_context(nc.sbuf_tensor("bint", [P, M3], F32))
    Abuf = ctx.enter_context(nc.sbuf_tensor("Abuf", [P, 2 * DC], F32))

    def sf(off, n):
        return scrF[:, off:off + n]

    def sb(off, n):
        return scrB[:, off:off + n]

    def row(off, n=TT):
        return rows[0:1, off:off + n]

    xf = lambda c: sf(c * TT, TT)
    xn = lambda c: sf(8192 + c * TT, TT)
    eb = lambda s: sf(0 + s * TT, TT)
    ob = lambda s: sf(2048 + s * TT, TT)
    cosb = sf(4096, 2048)
    sinb = sf(6144, 2048)
    u1 = sf(8192, 2048)
    u2 = sf(10240, 2048)
    rsb = sf(0, TT)
    p1b = sf(512, TT)
    p2b = sf(1024, TT)
    redt = sf(1536, 1)
    asb = sf(16384, TT)
    bsb = sf(16896, TT)
    a2sb = sf(17408, TT)
    b2sb = sf(17920, TT)
    Xn = lambda c, off, n: sb(c * NT + off, n)
    xbf = lambda c: sb(32768 + c * TT, TT)
    xsq = lambda c: sb(40960 + c * TT, TT)
    wsl = lambda i: sb(32768 + i * 2048, 2048)
    kvb = lambda par: sb(36864 + par * 2048, 2048)
    qsb = sb(40960, 2048)
    kt = lambda c: sb(c * TT, TT)
    vt = lambda c: sb(8192 + c * TT, TT)
    ksq = lambda c: sb(16384 + c * TT, TT)
    vsq = lambda c: sb(24576 + c * TT, TT)

    ps = [ctx.enter_context(nc.psum_tensor(f"ps{i}", [P, 512], F32)) for i in range(8)]
    pstat0, pstat1, pbc0, pbc1, pq0, pq1 = ps[0], ps[1], ps[2], ps[3], ps[4], ps[5]

    pr = Prog(nc)

    def t_init(v):
        v.memset(onesb[:, :], 1.0)
        v.memset(onesf[:, :], 1.0)
        v.memset(rows[0:1, 6143:6144], EPS)
        return v.memset(Abuf[:, :], 0.0)
    INIT = pr.add("vector", t_init, label="init")
    GBT = pr.dma("sync", gbt[:, :], gbin, label="gbt")
    BINT = pr.dma("sync", bint[:, :], bint_d, label="bint")

    SP1 = [None] * NTILE
    SP2 = [None] * NTILE
    XN2 = [None] * NTILE
    STAT2 = [None] * NTILE
    CAST3 = [None] * NTILE
    prevF1b = None
    prevBCE = None
    prevBCE2 = None
    prevBC_reader = None
    for i in range(NTILE):
        ti = slice(i * TT, (i + 1) * TT)

        def inap(t, i=i):
            return t[:, i * TT:(i + 1) * TT].rearrange("(c p) t -> p c t", p=P)

        L = pr.dma("sync", scrF[:, 0:8192].rearrange("p (c t) -> p c t", t=TT),
                   inap(x0), deps=[SP1[i - 1] if i else None], label=f"L{i}")

        def t_cast(s, i=i):
            s.activation(scrB[:, 32768:40960], scrF[:, 0:8192], AF.Identity)
            return s.activation(scrB[:, 40960:49152], scrF[:, 0:8192], AF.Square)
        CAST = pr.add("scalar", t_cast, deps=[L, STAT2[i - 1]], label=f"CAST{i}")

        def t_stat(t, i=i):
            for c in range(DC):
                t.matmul(pstat0[0:1, :], onesb[:, :], xbf(c), start=(c == 0), stop=(c == DC - 1))
            for c in range(DC):
                last = t.matmul(pstat1[0:1, :], onesb[:, :], xsq(c), start=(c == 0), stop=(c == DC - 1))
            return last
        STAT = pr.add("tensor", t_stat, deps=[CAST, INIT, prevF1b], label=f"STAT{i}")

        def mk_fin(pst0, pst1, ro):
            def t_f1(s):
                s.activation(row(ro), pst0[0:1, :], AF.Identity, scale=1.0 / D)
                s.activation(row(ro + 512), pst1[0:1, :], AF.Identity, scale=1.0 / D)
                return s.activation(row(ro + 1024), row(ro), AF.Square)
            def t_f2(v):
                return v.tensor_sub(row(ro + 1024), row(ro + 512), row(ro + 1024))
            def t_f3(s):
                return s.activation(row(ro + 1536), row(ro + 1024), AF.Sqrt, bias=rows[0:1, 6143:6144])
            def t_f4(v):
                v.reciprocal(row(ro + 2048), row(ro + 1536))
                v.drain()
                v.tensor_mul(row(ro + 2560), row(ro), row(ro + 2048))
                return v.tensor_scalar(row(ro + 2560), row(ro + 2560), -1.0, None, ALU.mult)
            return t_f1, t_f2, t_f3, t_f4

        f1, f2, f3, f4 = mk_fin(pstat0, pstat1, 0)
        F1 = pr.add("scalar", f1, deps=[STAT], label=f"F1_{i}")
        F2 = pr.add("vector", f2, deps=[F1], label=f"F2_{i}")
        F3 = pr.add("scalar", f3, deps=[F2], label=f"F3_{i}")
        F4 = pr.add("vector", f4, deps=[F3], label=f"F4_{i}")

        def t_bc(t):
            t.matmul(pbc0[:, :], onesf[0:1, :], row(2048), start=True, stop=True)
            return t.matmul(pbc1[:, :], onesf[0:1, :], row(2560), start=True, stop=True)
        BC = pr.add("tensor", t_bc, deps=[F4, prevBCE2], label=f"BC{i}")

        def t_bce(s):
            s.activation(asb, pbc0[:, :], AF.Identity)
            return s.activation(bsb, pbc1[:, :], AF.Identity)
        BCE = pr.add("scalar", t_bce, deps=[BC], label=f"BCE{i}")

        def t_xn(v):
            for c in range(DC):
                v.tensor_mul(xn(c), xf(c), asb)
                last = v.tensor_add(xn(c), xn(c), bsb)
            return last
        XNt = pr.add("vector", t_xn, deps=[BCE], label=f"XN{i}")

        def t_s01(v):
            for c in range(DC):
                last = v.tensor_add(xf(c), xf(c), xn(c))
            return last
        S01 = pr.add("vector", t_s01, deps=[XNt], label=f"S01_{i}")
        SP1[i] = pr.dma("sync", s01[:, ti.start:ti.stop].rearrange("(c p) t -> p c t", p=P),
                        scrF[:, 0:8192].rearrange("p (c t) -> p c t", t=TT),
                        deps=[S01], label=f"SP1_{i}")

        def t_cast3(s):
            s.activation(scrB[:, 32768:40960], scrF[:, 8192:16384], AF.Identity)
            return s.activation(scrB[:, 40960:49152], scrF[:, 8192:16384], AF.Square)
        CAST3[i] = pr.add("scalar", t_cast3, deps=[XNt, STAT], label=f"CAST3_{i}")
        SP2[i] = pr.dma("sync", xn0bf[:, ti.start:ti.stop].rearrange("(c p) t -> p c t", p=P),
                        scrB[:, 32768:40960].rearrange("p (c t) -> p c t", t=TT),
                        deps=[CAST3[i]], label=f"SP2_{i}")

        def t_stat2(t):
            for c in range(DC):
                t.matmul(pstat0[0:1, :], onesb[:, :], xbf(c), start=(c == 0), stop=(c == DC - 1))
            for c in range(DC):
                last = t.matmul(pstat1[0:1, :], onesb[:, :], xsq(c), start=(c == 0), stop=(c == DC - 1))
            return last
        STAT2[i] = pr.add("tensor", t_stat2, deps=[CAST3[i], F1], label=f"STAT2_{i}")

        g1, g2, g3, g4 = mk_fin(pstat0, pstat1, 3072)
        F1b = pr.add("scalar", g1, deps=[STAT2[i]], label=f"F1b{i}")
        F2b = pr.add("vector", g2, deps=[F1b], label=f"F2b{i}")
        F3b = pr.add("scalar", g3, deps=[F2b], label=f"F3b{i}")
        F4b = pr.add("vector", g4, deps=[F3b], label=f"F4b{i}")
        prevF1b = F1b

        def t_bc2(t):
            t.matmul(pbc0[:, :], onesf[0:1, :], row(3072 + 2048), start=True, stop=True)
            return t.matmul(pbc1[:, :], onesf[0:1, :], row(3072 + 2560), start=True, stop=True)
        BC2 = pr.add("tensor", t_bc2, deps=[F4b, BCE], label=f"BC2_{i}")

        def t_bce2(s):
            s.activation(a2sb, pbc0[:, :], AF.Identity)
            return s.activation(b2sb, pbc1[:, :], AF.Identity)
        BCE2 = pr.add("scalar", t_bce2, deps=[BC2, XN2[i - 1] if i else None], label=f"BCE2_{i}")
        prevBCE2 = BCE2

        def t_xn2(v, i=i):
            for c in range(DC):
                v.tensor_mul(xn(c), xn(c), a2sb)
                v.tensor_add(xn(c), xn(c), b2sb)
                last = v.tensor_scalar(Xn(c, i * TT, TT), xn(c), gbt[:, c:c + 1],
                                       gbt[:, DC + c:DC + c + 1], ALU.mult, ALU.add)
            return last
        XN2[i] = pr.add("vector", t_xn2, deps=[BCE2, GBT, SP1[i]], label=f"XN2_{i}")

    WL = [None] * M3
    bank_reader = {0: None, 1: None}
    wsl_reader = {0: None, 1: None}
    EVlast = [None] * M3
    qsb_reader = None
    kvb_reader = {0: None, 1: None}
    cos_reader = None
    eb_reader = None
    SPILLS = []
    MMlast = None
    pairE = {}
    ROTS = {}

    for m in range(M3):
        slot = m % 2

        def t_wl(g, m=m, slot=slot):
            return g.dma_start(
                out=wsl(slot).rearrange("p (kc mm) -> p kc mm", mm=P),
                in_=winp[:, m * P:(m + 1) * P].rearrange("(kc p) mm -> p kc mm", p=P))
        wl_deps = [wsl_reader[slot]]
        if m < 2:
            wl_deps += [STAT2[NTILE - 1], CAST3[NTILE - 1]]
        WL[m] = pr.add("gpsimd", t_wl, deps=wl_deps, step=16, label=f"WL{m}")

        if m < 16:
            kind = "Q"
        else:
            c = (m - 16) % 16
            kv = 0 if m < 32 else 1
            kind = "ROT_E" if (c < 8 and c % 2 == 0) else ("ROT_O" if c < 8 else "PASS")

        for sub in range(4):
            pq = pq0 if sub % 2 == 0 else pq1

            def t_mm(t, m=m, sub=sub, pq=pq):
                for kc in range(DC):
                    last = t.matmul(pq[:, :], wsl(m % 2)[:, kc * P:(kc + 1) * P],
                                    Xn(kc, sub * TT, TT), start=(kc == 0), stop=(kc == DC - 1))
                return last
            mm_deps = [WL[m], bank_reader[sub % 2]]
            if m == 0 and sub == 0:
                mm_deps += [XN2[j] for j in range(NTILE)]
            MM = pr.add("tensor", t_mm, deps=mm_deps, label=f"MM{m}_{sub}")
            MMlast = MM
            wsl_reader[m % 2] = MM

            if kind == "Q":
                def t_ev(s, m=m, sub=sub, pq=pq):
                    return s.activation(qsb[:, sub * TT:(sub + 1) * TT], pq[:, :],
                                        AF.Identity, bias=bint[:, m:m + 1])
                ev_deps = [MM, BINT, qsb_reader]
            elif kind == "ROT_E":
                def t_ev(s, m=m, sub=sub, pq=pq):
                    return s.activation(eb(sub), pq[:, :], AF.Identity, bias=bint[:, m:m + 1])
                ev_deps = [MM, BINT, eb_reader]
            elif kind == "ROT_O":
                def t_ev(s, m=m, sub=sub, pq=pq):
                    return s.activation(ob(sub), pq[:, :], AF.Identity, bias=bint[:, m:m + 1])
                ev_deps = [MM, BINT, eb_reader]
            else:
                def t_ev(s, m=m, sub=sub, pq=pq):
                    return s.activation(kvb(0)[0:P, sub * TT:(sub + 1) * TT], pq[:, :],
                                        AF.Identity, bias=bint[:, m:m + 1])
                ev_deps = [MM, BINT, kvb_reader[0]]
            EV = pr.add("scalar", t_ev, deps=ev_deps, label=f"EV{m}_{sub}")
            bank_reader[sub % 2] = EV
        EVlast[m] = EV

        if kind == "Q":
            SQ = pr.dma("sync", qbf[m * P:(m + 1) * P, :], qsb, deps=[EV], label=f"SQ{m}")
            qsb_reader = SQ
            SPILLS.append(SQ)
        elif kind == "ROT_E":
            pairE[m] = EV
            a = ((m - 16) % 16) // 2
            COSL = pr.dma("sync", cosb, cosT[a * P:(a + 1) * P, :],
                          deps=[cos_reader], label=f"COSL{m}")
            SINL = pr.dma("sync", sinb, sinT[a * P:(a + 1) * P, :],
                          deps=[cos_reader], label=f"SINL{m}")
            ROTS[("cos", m)] = (COSL, SINL)
        elif kind == "ROT_O":
            COSL, SINL = ROTS[("cos", m - 1)]

            def t_rot(v, m=m):
                ebf = scrF[:, 0:2048]
                obf = scrF[:, 2048:4096]
                v.tensor_mul(u1, ebf, cosb)
                v.tensor_mul(u2, obf, sinb)
                v.tensor_sub(kvb(0), u1, u2)
                v.tensor_mul(u1, obf, cosb)
                v.tensor_mul(u2, ebf, sinb)
                return v.tensor_add(kvb(1), u1, u2)
            ROT = pr.add("vector", t_rot,
                         deps=[pairE[m - 1], EV, COSL, SINL, kvb_reader[0], kvb_reader[1]],
                         label=f"ROT{m}")
            cos_reader = ROT
            eb_reader = ROT
            kv = 0 if m < 32 else 1
            c = (m - 16) % 16
            SPE = pr.dma("sync", kvsp[kv, c - 1], kvb(0), deps=[ROT], label=f"SPE{m}")
            SPO = pr.dma("sync", kvsp[kv, c], kvb(1), deps=[ROT], label=f"SPO{m}")
            kvb_reader[0] = SPE
            kvb_reader[1] = SPO
            SPILLS += [SPE, SPO]
        else:
            kv = 0 if m < 32 else 1
            c = (m - 16) % 16
            SPP = pr.dma("sync", kvsp[kv, c], kvb(0), deps=[EV], label=f"SPP{m}")
            kvb_reader[0] = SPP
            SPILLS.append(SPP)

    prevPRD = None
    PRD = None
    for i in range(NTILE):
        kl_deps = [SPILLS[-1], MMlast, prevPRD]
        KL1 = pr.dma("sync", scrB[:, 0:8192].rearrange("p (c t) -> p c t", t=TT),
                     kvsp[0][:, :, i * TT:(i + 1) * TT].rearrange("c p t -> p c t"),
                     deps=kl_deps, label=f"KLa{i}")
        KL = pr.dma("sync", scrB[:, 8192:16384].rearrange("p (c t) -> p c t", t=TT),
                    kvsp[1][:, :, i * TT:(i + 1) * TT].rearrange("c p t -> p c t"),
                    deps=kl_deps, label=f"KLb{i}")

        def t_csq(s):
            s.activation(scrB[:, 16384:24576], scrB[:, 0:8192], AF.Square)
            return s.activation(scrB[:, 24576:32768], scrB[:, 8192:16384], AF.Square)
        csq_deps = [KL1, KL, prevPRD]
        if i == 0:
            csq_deps.append(EVlast[M3 - 1])
        CSQ = pr.add("scalar", t_csq, deps=csq_deps, label=f"CSQ{i}")

        def t_nmm(t):
            for c in range(DC):
                t.matmul(pstat0[0:1, :], onesb[:, :], ksq(c), start=(c == 0), stop=(c == DC - 1))
            for c in range(DC):
                last = t.matmul(pstat1[0:1, :], onesb[:, :], vsq(c), start=(c == 0), stop=(c == DC - 1))
            return last
        NMM = pr.add("tensor", t_nmm, deps=[CSQ], label=f"NMM{i}")

        def t_fc1(s):
            s.activation(row(0), pstat0[0:1, :], AF.Identity)
            return s.activation(row(512), pstat1[0:1, :], AF.Identity)
        FC1 = pr.add("scalar", t_fc1, deps=[NMM], label=f"FC1_{i}")

        def t_fc2(v):
            return v.tensor_mul(row(1024), row(0), row(512))
        FC2 = pr.add("vector", t_fc2, deps=[FC1], label=f"FC2_{i}")

        def t_fc3(s):
            return s.activation(row(1536), row(1024), AF.Sqrt)
        FC3 = pr.add("scalar", t_fc3, deps=[FC2], label=f"FC3_{i}")

        def t_fc4(v):
            return v.reciprocal(row(2048), row(1536))
        FC4 = pr.add("vector", t_fc4, deps=[FC3], label=f"FC4_{i}")

        def t_rbc(t):
            return t.matmul(pbc0[:, :], onesf[0:1, :], row(2048), start=True, stop=True)
        RBC = pr.add("tensor", t_rbc, deps=[FC4], label=f"RBC{i}")

        def t_rbce(s):
            return s.activation(rsb, pbc0[:, :], AF.Identity)
        RBCE = pr.add("scalar", t_rbce, deps=[RBC, prevPRD], label=f"RBCE{i}")

        def t_prd(v, i=i):
            for c in range(DC):
                v.tensor_mul(p1b, kt(c), vt(c))
                v.tensor_mul(p1b, p1b, rsb)
                v.reduce_sum(redt, p1b, axis=AX)
                v.drain()
                last = v.tensor_add(Abuf[:, c:c + 1], Abuf[:, c:c + 1], redt)
            return last
        PRD = pr.add("vector", t_prd, deps=[RBCE, CSQ, INIT], label=f"PRD{i}")
        prevPRD = PRD

    SA = pr.dma("sync", apart[:, 0:DC], Abuf[:, 0:DC], deps=[PRD], label="SA")

    def t_fence(v):
        return v.nop()
    pr.add("vector", t_fence, deps=[SA, SP1[NTILE - 1], SP2[NTILE - 1]] + SPILLS[-1:],
           label="fence")
    return pr


def host_inputs_p1(X, W_in, b_in, gamma, beta, core):
    kp = kperm_local()
    colperm = np.concatenate([np.arange(2048), 2048 + kp, 4096 + kp])
    x = core_slice_tokens(X, core)
    pos = core_positions(core)
    cosA, sinA = rot_tables(pos)
    gb = np.stack([gamma.reshape(DC, P), beta.reshape(DC, P)], axis=0)
    gbin = np.concatenate([gb[0].T, gb[1].T], axis=1)
    winp = W_in[:, colperm]
    binp = b_in[colperm].reshape(M3, P).T
    return {
        "x0": f32c(x.T),
        "winp": f32c(winp),
        "binp": f32c(binp),
        "cosT": f32c(cosA),
        "sinT": f32c(sinA),
        "gbin": f32c(gbin),
    }



"""Launch 2 (P2): A-combine; u = WHT1(xn0*SU_ug); up/gate + silu*SU_dp -> h with
WHT2 stage-A applied -> hsp spill; per token-tile: WHT2 swap-reload + stage-B +
down-proj -> ffn0; Y = (A*Q)@W_out + b_out; x1 = s01 + Y + ffn0; LN1 -> xn1;
then q/k/v projections for L1. Feature-major [d-chunk(128p), tokens] everywhere;
v1 output is token-major."""

P = 128
NT = 2048
TT = 512
NTILE = NT // TT
D = 2048
DC = 16
FC = 64
AX = mybir.AxisListType.X
RS2048 = float(1.0 / np.sqrt(np.float64(2048.0)))
RS8192 = float(1.0 / np.sqrt(np.float64(8192.0)))
EPS = 1e-5


def build_p2(nc, ctx):
    s01 = nc.dram_tensor("s01", [D, NT], F32, kind="ExternalInput").ap()
    xn0bf = nc.dram_tensor("xn0bf", [D, NT], BF16, kind="ExternalInput").ap()
    qbf = nc.dram_tensor("qbf", [D, NT], BF16, kind="ExternalInput").ap()
    apc = nc.dram_tensor("apc", [2, P, DC], F32, kind="ExternalInput").ap()
    such = nc.dram_tensor("such", [P, DC], F32, kind="ExternalInput").ap()
    sudp = nc.dram_tensor("sudp", [P, FC], F32, kind="ExternalInput").ap()
    scup_d = nc.dram_tensor("scup", [P, FC], F32, kind="ExternalInput").ap()
    scgt_d = nc.dram_tensor("scgt", [P, FC], F32, kind="ExternalInput").ap()
    scdp_d = nc.dram_tensor("scdp", [P, DC], F32, kind="ExternalInput").ap()
    h128_d = nc.dram_tensor("h128", [P, P], BF16, kind="ExternalInput").ap()
    hsb1_d = nc.dram_tensor("hsb1", [P, P], BF16, kind="ExternalInput").ap()
    hsb2_d = nc.dram_tensor("hsb2", [P, P], BF16, kind="ExternalInput").ap()
    wupT = nc.dram_tensor("wupT", [D, 8192], F32, kind="ExternalInput").ap()
    wgtT = nc.dram_tensor("wgtT", [D, 8192], F32, kind="ExternalInput").ap()
    wdnT = nc.dram_tensor("wdnT", [8192, D], F32, kind="ExternalInput").ap()
    wout = nc.dram_tensor("wout", [D, D], F32, kind="ExternalInput").ap()
    bout_d = nc.dram_tensor("bout", [P, DC], F32, kind="ExternalInput").ap()
    wq = nc.dram_tensor("wq", [D, D], F32, kind="ExternalInput").ap()
    wk = nc.dram_tensor("wk", [D, 512], F32, kind="ExternalInput").ap()
    wv = nc.dram_tensor("wv", [D, 512], F32, kind="ExternalInput").ap()
    x1 = nc.dram_tensor("x1", [D, NT], F32, kind="ExternalOutput").ap()
    xn1bf = nc.dram_tensor("xn1bf", [D, NT], BF16, kind="ExternalOutput").ap()
    q1 = nc.dram_tensor("q1", [D, NT], BF16, kind="ExternalOutput").ap()
    k1 = nc.dram_tensor("k1", [512, NT], BF16, kind="ExternalOutput").ap()
    v1 = nc.dram_tensor("v1", [NT, 512], BF16, kind="ExternalOutput").ap()
    w1u = nc.dram_tensor("w1u", [DC, P, NT], BF16).ap()
    hsp = nc.dram_tensor("hsp", [FC, P, NT], BF16).ap()
    wdnb = nc.dram_tensor("wdnb", [FC, P, D], BF16).ap()

    scrB = ctx.enter_context(nc.sbuf_tensor("scrB", [P, 57344], BF16))
    scrF = ctx.enter_context(nc.sbuf_tensor("scrF", [P, 18432], F32))
    rows = ctx.enter_context(nc.sbuf_tensor("rows", [1, 4096], F32))
    consts = ctx.enter_context(nc.sbuf_tensor("consts", [P, 3 * P], BF16))
    onesb = ctx.enter_context(nc.sbuf_tensor("onesb", [P, 1], BF16))
    onesf = ctx.enter_context(nc.sbuf_tensor("onesf", [1, P], F32))
    sc = ctx.enter_context(nc.sbuf_tensor("sc", [P, 256], F32))

    def sB(off, n):
        return scrB[:, off:off + n]

    def sF(off, n):
        return scrF[:, off:off + n]

    def row(off, n=TT):
        return rows[0:1, off:off + n]

    h128 = consts[:, 0:P]
    hsb1 = consts[:, P:2 * P]
    hsb2 = consts[:, 2 * P:3 * P]
    O_AC, O_BO, O_SCUP, O_SCGT, O_SUDP, O_SUCH, O_SCDP = 0, 16, 32, 96, 160, 224, 240

    def scc(base, j):
        return sc[:, base + j:base + j + 1]

    ps = [ctx.enter_context(nc.psum_tensor(f"ps{i}", [P, 512], F32)) for i in range(8)]

    pr = Prog(nc)
    ev = {}

    def t_init(v):
        v.memset(onesb[:, :], 1.0)
        v.memset(rows[0:1, 4095:4096], EPS)
        return v.memset(onesf[:, :], 1.0)
    INIT = pr.add("vector", t_init, label="init")
    LH = [pr.dma("sync", h128, h128_d, label="h128"),
          pr.dma("sync", hsb1, hsb1_d, label="hsb1"),
          pr.dma("sync", hsb2, hsb2_d, label="hsb2")]
    LSC = [pr.dma("sync", sc[:, O_AC:O_AC + DC], apc[0], label="ap0"),
           pr.dma("sync", sc[:, O_BO:O_BO + DC], bout_d, label="bout"),
           pr.dma("sync", sc[:, O_SCUP:O_SCUP + FC], scup_d, label="scup"),
           pr.dma("sync", sc[:, O_SCGT:O_SCGT + FC], scgt_d, label="scgt"),
           pr.dma("sync", sc[:, O_SUDP:O_SUDP + FC], sudp, label="sudp"),
           pr.dma("sync", sc[:, O_SUCH:O_SUCH + DC], such, label="such"),
           pr.dma("sync", sc[:, O_SCDP:O_SCDP + DC], scdp_d, label="scdp")]
    AP1 = pr.dma("sync", sF(0, DC), apc[1], label="ap1")

    def t_acmb(v):
        v.tensor_add(sc[:, O_AC:O_AC + DC], sc[:, O_AC:O_AC + DC], sF(0, DC))
        v.tensor_scalar(sc[:, O_AC:O_AC + DC], sc[:, O_AC:O_AC + DC], 0.5, 0.5,
                        ALU.mult, ALU.add)
        v.tensor_scalar(sc[:, O_SCUP:O_SCUP + FC], sc[:, O_SCUP:O_SCUP + FC],
                        RS2048, None, ALU.mult)
        v.tensor_scalar(sc[:, O_SCGT:O_SCGT + FC], sc[:, O_SCGT:O_SCGT + FC],
                        RS2048, None, ALU.mult)
        return v.tensor_scalar(sc[:, O_SCDP:O_SCDP + DC], sc[:, O_SCDP:O_SCDP + DC],
                               RS8192, None, ALU.mult)
    ACMB = pr.add("vector", t_acmb, deps=LSC + [AP1], label="acmb")

    SP1U = []
    prevSA = None
    for i in range(NTILE):
        ti = slice(i * TT, (i + 1) * TT)
        LX = pr.dma("sync", scrB[:, 32768:40960].rearrange("p (c t) -> p c t", t=TT),
                    xn0bf[:, ti.start:ti.stop].rearrange("(c p) t -> p c t", p=P),
                    deps=[prevSA], label=f"uLX{i}")

        def t_su(v):
            for c in range(DC):
                last = v.tensor_scalar(sB(32768 + c * TT, TT), sB(32768 + c * TT, TT),
                                       scc(O_SUCH, c), None, ALU.mult)
            return last
        SU = pr.add("vector", t_su, deps=[LX, ACMB], label=f"uSU{i}")

        lastEV = None
        for c in range(DC):
            bank = c % 2

            def t_mm(t, c=c, bank=bank):
                return t.matmul(ps[bank][:, :], h128, sB(32768 + c * TT, TT),
                                start=True, stop=True)
            MM = pr.add("tensor", t_mm, deps=[SU, LH[0], ev.get(("ps", bank))],
                        label=f"uA{i}_{c}")

            def t_ev(s, c=c, bank=bank):
                return s.activation(sB(40960 + c * TT, TT), ps[bank][:, :], AF.Identity)
            EV = pr.add("scalar", t_ev, deps=[MM], label=f"uAe{i}_{c}")
            ev[("ps", bank)] = EV
            lastEV = EV
        SA = pr.dma("sync", w1u[:, :, ti.start:ti.stop].rearrange("c p t -> p c t"),
                    scrB[:, 40960:49152].rearrange("p (c t) -> p c t", t=TT),
                    deps=[lastEV], label=f"uSA{i}")
        prevSA = SA
        SP1U.append(SA)

    RLU = []
    for r in range(8):
        dst = scrB[16 * r:16 * (r + 1), 0:32768].rearrange("q (m t) -> q m t", t=NT)
        src = w1u.rearrange("c (m r2) t -> r2 c m t", r2=8)[r]
        RLU.append(pr.dma("sync", dst, src, deps=SP1U, label=f"uRL{r}"))

    ULAST = None
    for m in range(DC):
        for s in range(4):
            bank = (m * 4 + s) % 2

            def t_mm(t, m=m, s=s, bank=bank):
                return t.matmul(ps[bank][:, :], hsb1,
                                sB(m * NT + s * TT, TT), start=True, stop=True)
            MM = pr.add("tensor", t_mm, deps=RLU + [LH[1], ev.get(("ps", bank))],
                        label=f"uB{m}_{s}")

            def t_ev(sx, m=m, s=s, bank=bank):
                return sx.activation(sB(m * NT + s * TT, TT), ps[bank][:, :], AF.Identity)
            EV = pr.add("scalar", t_ev, deps=[MM], label=f"uBe{m}_{s}")
            ev[("ps", bank)] = EV
            ULAST = EV

    fup = lambda l: sB(32768 + l * 4096, 2048)
    fgt = lambda l: sB(32768 + l * 4096 + 2048, 2048)
    hbuf = sB(40960, 2048)
    wdc = lambda l: sB(43008 + l * 2048, 2048)
    xupf = lambda s: sF(s * TT, TT)
    sgf = lambda s: sF(2048 + s * TT, TT)

    wsl_rd = {0: None, 1: None}
    wdc_rd = {0: None, 1: None}
    hb_rd = None
    prevHS = None
    for j in range(FC):
        l = j % 2
        WL = pr.dma("gpsimd", fup(l).rearrange("p (kc m) -> p kc m", m=P),
                    wupT[:, j * P:(j + 1) * P].rearrange("(kc p) m -> p kc m", p=P),
                    deps=[wsl_rd[l]] + ([ULAST] if j < 2 else []), label=f"aWU{j}")
        WG = pr.dma("gpsimd", fgt(l).rearrange("p (kc m) -> p kc m", m=P),
                    wgtT[:, j * P:(j + 1) * P].rearrange("(kc p) m -> p kc m", p=P),
                    deps=[wsl_rd[l]] + ([ULAST] if j < 2 else []), label=f"aWG{j}")
        WD = pr.dma("gpsimd", wdc(l), wdnT[j * P:(j + 1) * P, :],
                    deps=[wdc_rd[l]] + ([ULAST] if j < 2 else []), label=f"aWD{j}")
        WDS = pr.dma("sync", wdnb[j], wdc(l), deps=[WD], label=f"aWDS{j}")
        wdc_rd[l] = WDS

        THs = []
        for s in range(4):
            bu, bg = 4 + (s % 2), 6 + (s % 2)

            def t_up(t, j=j, s=s, l=l, bu=bu):
                for kc in range(DC):
                    last = t.matmul(ps[bu][:, :], fup(l)[:, kc * P:(kc + 1) * P],
                                    sB(kc * NT + s * TT, TT),
                                    start=(kc == 0), stop=(kc == DC - 1))
                return last
            MU = pr.add("tensor", t_up, deps=[WL, ULAST, ev.get(("ps", bu))], label=f"aU{j}_{s}")

            def t_gt(t, j=j, s=s, l=l, bg=bg):
                for kc in range(DC):
                    last = t.matmul(ps[bg][:, :], fgt(l)[:, kc * P:(kc + 1) * P],
                                    sB(kc * NT + s * TT, TT),
                                    start=(kc == 0), stop=(kc == DC - 1))
                return last
            MG = pr.add("tensor", t_gt, deps=[WG, ev.get(("ps", bg))], label=f"aG{j}_{s}")

            def t_evu(sx, j=j, s=s, bu=bu):
                return sx.activation(xupf(s), ps[bu][:, :], AF.Identity,
                                     scale=scc(O_SCUP, j))
            EU = pr.add("scalar", t_evu, deps=[MU, ACMB, ev.get(("xup", s))], label=f"aEU{j}_{s}")
            ev[("ps", bu)] = EU

            def t_evg(sx, j=j, s=s, bg=bg):
                return sx.activation(sgf(s), ps[bg][:, :], AF.Silu,
                                     scale=scc(O_SCGT, j))
            EG = pr.add("scalar", t_evg, deps=[MG, ev.get(("xup", s))], label=f"aEG{j}_{s}")
            ev[("ps", bg)] = EG

            def t_h(v, j=j, s=s):
                return v.scalar_tensor_tensor(hbuf[:, s * TT:(s + 1) * TT], sgf(s),
                                              scc(O_SUDP, j), xupf(s), ALU.mult, ALU.mult)
            TH = pr.add("vector", t_h, deps=[EU, EG, hb_rd], label=f"aH{j}_{s}")
            ev[("xup", s)] = TH
            THs.append(TH)

        SAEV = []
        for s in range(4):
            bank = s % 2

            def t_mm(t, s=s, bank=bank):
                return t.matmul(ps[bank][:, :], h128, hbuf[:, s * TT:(s + 1) * TT],
                                start=True, stop=True)
            MM = pr.add("tensor", t_mm, deps=[THs[s], LH[0], ev.get(("ps", bank))],
                        label=f"aA{j}_{s}")

            def t_ev(sx, s=s, bank=bank):
                return sx.activation(hbuf[:, s * TT:(s + 1) * TT], ps[bank][:, :], AF.Identity)
            EV = pr.add("scalar", t_ev, deps=[MM], label=f"aAe{j}_{s}")
            ev[("ps", bank)] = EV
            SAEV.append(EV)
        HS = pr.dma("sync", hsp[j], hbuf, deps=SAEV, label=f"aHS{j}")
        hb_rd = HS
        wsl_rd[l] = SAEV[-1]
        prevHS = HS

    prevLN = None
    prevQspill = None
    prevX1 = None
    prevDNlast = None
    prevYlast = None
    xn1_tiles = []
    for i in range(NTILE):
        ti = slice(i * TT, (i + 1) * TT)
        RL = []
        for r in range(2):
            dst = scrB[64 * r:64 * (r + 1), 0:32768].rearrange("q (m t) -> q m t", t=TT)
            src = hsp[:, :, ti.start:ti.stop].rearrange("c (m r2) t -> r2 c m t", r2=2)[r]
            RL.append(pr.dma("sync", dst, src,
                             deps=[prevHS if i == 0 else prevDNlast], label=f"bRL{i}_{r}"))

        SBE = []
        for m in range(FC):
            bank = m % 2

            def t_mm(t, m=m, bank=bank):
                return t.matmul(ps[bank][:, :], hsb2, sB(m * TT, TT), start=True, stop=True)
            MM = pr.add("tensor", t_mm, deps=RL + [LH[2], ev.get(("ps", bank))],
                        label=f"bB{i}_{m}")

            def t_ev(sx, m=m, bank=bank):
                return sx.activation(sB(m * TT, TT), ps[bank][:, :], AF.Identity)
            EV = pr.add("scalar", t_ev, deps=[MM], label=f"bBe{i}_{m}")
            ev[("ps", bank)] = EV
            SBE.append(EV)

        DNE = []
        wdns_rd = {0: None, 1: None}
        MMl = None
        for g in range(2):
            for m in range(FC):
                l = m % 2
                wld_deps = [wdns_rd[l]]
                if g == 0 and m < 2:
                    wld_deps += [wdc_rd[0], wdc_rd[1], ev.get(("wdcY", l))]
                WLD = pr.dma("sync", wdc(l)[:, 0:1024], wdnb[m, :, g * 1024:(g + 1) * 1024],
                             deps=wld_deps, label=f"bWD{i}_{g}_{m}")
                for oc in range(8):
                    mmdeps = [WLD, SBE[-1]]
                    if m == 0:
                        mmdeps.append(prevLN)
                        mmdeps.append(ev.get(("ps", 0)))
                        mmdeps.append(ev.get(("ps", 1)))
                        if g == 1:
                            mmdeps.append(DNE[-1] if DNE else None)

                    def t_mm(t, m=m, g=g, oc=oc, l=l):
                        return t.matmul(ps[oc][:, :], wdc(l)[:, oc * P:(oc + 1) * P],
                                        sB(m * TT, TT), start=(m == 0), stop=(m == FC - 1))
                    MMl = pr.add("tensor", t_mm, deps=mmdeps, label=f"bD{i}_{g}_{m}_{oc}")
                wdns_rd[l] = MMl
            for oc in range(8):
                def t_ev(sx, g=g, oc=oc):
                    return sx.activation(sF((g * 8 + oc) * TT, TT), ps[oc][:, :],
                                         AF.Identity, scale=scc(O_SCDP, g * 8 + oc))
                EVD = pr.add("scalar", t_ev, deps=[MMl, ev.get(("ffn", g * 8 + oc))],
                             label=f"bDe{i}_{g}_{oc}")
                DNE.append(EVD)
        prevDNlast = DNE[-1]

        LQ = pr.dma("sync", scrB[:, 32768:40960].rearrange("p (c t) -> p c t", t=TT),
                    qbf[:, ti.start:ti.stop].rearrange("(c p) t -> p c t", p=P),
                    deps=[wdns_rd[0], wdns_rd[1], prevQspill], label=f"bLQ{i}")
        LS = pr.dma("sync", scrF[:, 8192:16384].rearrange("p (c t) -> p c t", t=TT),
                    s01[:, ti.start:ti.stop].rearrange("(c p) t -> p c t", p=P),
                    deps=[prevX1, prevLN], label=f"bLS{i}")

        def t_aq(v):
            for c in range(DC):
                last = v.tensor_scalar(sB(32768 + c * TT, TT), sB(32768 + c * TT, TT),
                                       scc(O_AC, c), None, ALU.mult)
            return last
        AQ = pr.add("vector", t_aq, deps=[LQ, ACMB], label=f"bAQ{i}")

        X1parts = []
        wo_rd = {0: None, 1: None}
        for oc in range(DC):
            l = oc % 2
            bank = 2 + oc % 2
            wo_deps = [wo_rd[l]]
            if oc < 2:
                wo_deps += [wdns_rd[0], wdns_rd[1]]
            WO = pr.dma("gpsimd", wdc(l).rearrange("p (kc m) -> p kc m", m=P),
                        wout[:, oc * P:(oc + 1) * P].rearrange("(kc p) m -> p kc m", p=P),
                        deps=wo_deps, label=f"bWO{i}_{oc}")

            def t_mm(t, oc=oc, l=l, bank=bank):
                for kc in range(DC):
                    last = t.matmul(ps[bank][:, :], wdc(l)[:, kc * P:(kc + 1) * P],
                                    sB(32768 + kc * TT, TT),
                                    start=(kc == 0), stop=(kc == DC - 1))
                return last
            MMY = pr.add("tensor", t_mm,
                         deps=[WO, AQ, ev.get(("yps", bank)), DNE[-1] if oc < 2 else None],
                         label=f"bY{i}_{oc}")

            def t_ev(sx, oc=oc, bank=bank):
                return sx.activation(sF(16384 + (oc % 2) * TT, TT), ps[bank][:, :],
                                     AF.Identity, bias=scc(O_BO, oc))
            EVY = pr.add("scalar", t_ev,
                         deps=[MMY, ev.get(("yb", oc % 2)), prevLN if oc < 2 else None],
                         label=f"bYe{i}_{oc}")

            def t_x1(v, oc=oc):
                v.tensor_add(sF(16384 + (oc % 2) * TT, TT), sF(16384 + (oc % 2) * TT, TT),
                             sF(oc * TT, TT))
                return v.tensor_add(sF(8192 + oc * TT, TT), sF(8192 + oc * TT, TT),
                                    sF(16384 + (oc % 2) * TT, TT))
            TX = pr.add("vector", t_x1, deps=[EVY, LS, DNE[-1]], label=f"bX1_{i}_{oc}")
            ev[("yps", bank)] = TX
            ev[("yb", oc % 2)] = TX
            ev[("ffn", oc)] = TX
            X1parts.append(TX)
            wo_rd[l] = MMY
            ev[("wdcY", l)] = MMY
        prevYlast = X1parts[-1]

        SX1 = pr.dma("sync", x1[:, ti.start:ti.stop].rearrange("(c p) t -> p c t", p=P),
                     scrF[:, 8192:16384].rearrange("p (c t) -> p c t", t=TT),
                     deps=[X1parts[-1]], label=f"bSX1_{i}")
        prevX1 = SX1

        def t_cast(sx):
            sx.activation(scrB[:, 40960:49152], scrF[:, 8192:16384], AF.Identity)
            return sx.activation(scrB[:, 49152:57344], scrF[:, 8192:16384], AF.Square)
        CA = pr.add("scalar", t_cast, deps=[X1parts[-1], prevQspill], label=f"bCA{i}")

        def t_stat(t):
            for c in range(DC):
                t.matmul(ps[4][0:1, :], onesb[:, :], sB(40960 + c * TT, TT),
                         start=(c == 0), stop=(c == DC - 1))
            for c in range(DC):
                last = t.matmul(ps[5][0:1, :], onesb[:, :], sB(49152 + c * TT, TT),
                                start=(c == 0), stop=(c == DC - 1))
            return last
        ST = pr.add("tensor", t_stat, deps=[CA, INIT, ev.get("lnps")], label=f"bST{i}")

        def t_f1(sx):
            sx.activation(row(0), ps[4][0:1, :], AF.Identity, scale=1.0 / D)
            sx.activation(row(512), ps[5][0:1, :], AF.Identity, scale=1.0 / D)
            return sx.activation(row(1024), row(0), AF.Square)
        F1 = pr.add("scalar", t_f1, deps=[ST], label=f"bF1{i}")
        ev["lnps"] = F1

        def t_f2(v):
            return v.tensor_sub(row(1024), row(512), row(1024))
        F2 = pr.add("vector", t_f2, deps=[F1], label=f"bF2{i}")

        def t_f3(sx):
            return sx.activation(row(1536), row(1024), AF.Sqrt, bias=rows[0:1, 4095:4096])
        F3 = pr.add("scalar", t_f3, deps=[F2], label=f"bF3{i}")

        def t_f4(v):
            v.reciprocal(row(2048), row(1536))
            v.drain()
            v.tensor_mul(row(2560), row(0), row(2048))
            return v.tensor_scalar(row(2560), row(2560), -1.0, None, ALU.mult)
        F4 = pr.add("vector", t_f4, deps=[F3], label=f"bF4{i}")

        def t_bc(t):
            t.matmul(ps[6][:, :], onesf[0:1, :], row(2048), start=True, stop=True)
            return t.matmul(ps[7][:, :], onesf[0:1, :], row(2560), start=True, stop=True)
        BC = pr.add("tensor", t_bc, deps=[F4, ev.get("bcps")], label=f"bBC{i}")

        def t_bce(sx):
            sx.activation(sF(16384, TT), ps[6][:, :], AF.Identity)
            return sx.activation(sF(16384 + TT, TT), ps[7][:, :], AF.Identity)
        BCE = pr.add("scalar", t_bce, deps=[BC, prevYlast], label=f"bBCE{i}")

        def t_xn1(v):
            for c in range(DC):
                v.tensor_mul(sF(16384 + 2 * TT, TT), sF(8192 + c * TT, TT), sF(16384, TT))
                last = v.tensor_add(sB(40960 + c * TT, TT), sF(16384 + 2 * TT, TT),
                                    sF(16384 + TT, TT))
            return last
        XN1 = pr.add("vector", t_xn1, deps=[BCE, CA], label=f"bXN1_{i}")
        prevLN = XN1
        ev["bcps"] = XN1

        SXN = pr.dma("sync", xn1bf[:, ti.start:ti.stop].rearrange("(c p) t -> p c t", p=P),
                     scrB[:, 40960:49152].rearrange("p (c t) -> p c t", t=TT),
                     deps=[XN1], label=f"bSXN{i}")
        xn1_tiles.append(SXN)
        prevQspill = SXN

    RXN = []
    for i in range(NTILE):
        dst = scrB[:, 0:32768].rearrange("p (c x) -> p c x", c=DC)[:, :, i * TT:(i + 1) * TT]
        RXN.append(pr.dma("sync", dst,
                          xn1bf[:, i * TT:(i + 1) * TT].rearrange("(c p) t -> p c t", p=P),
                          deps=[xn1_tiles[i], prevDNlast, prevYlast, prevLN],
                          label=f"cRX{i}"))

    cws = lambda l: sB(32768 + l * 2048, 2048)
    cev = sB(40960, 2048)
    cw_rd = {0: None, 1: None}
    cev_rd = prevQspill
    lastSO = None
    for idx in range(20):
        l = idx % 2
        if idx < 16:
            wsrc = wq[:, idx * P:(idx + 1) * P]
            dst = q1[idx * P:(idx + 1) * P, :]
        else:
            kk = idx - 16
            wsrc = wk[:, kk * P:(kk + 1) * P]
            dst = k1[kk * P:(kk + 1) * P, :]
        wdeps = [cw_rd[l]]
        if idx < 2:
            wdeps += [prevLN, ev.get(("yps", 2)), ev.get(("yps", 3))]
        WL = pr.dma("gpsimd", cws(l).rearrange("p (kc m) -> p kc m", m=P),
                    wsrc.rearrange("(kc p) m -> p kc m", p=P),
                    deps=wdeps, label=f"cW{idx}")
        EVl = []
        MMl = None
        for s in range(4):
            bank = s % 2
            mmdeps = [WL, ev.get(("cps", bank))]
            if idx == 0 and s < 2:
                mmdeps += RXN

            def t_mm(t, idx=idx, s=s, l=l, bank=bank):
                for kc in range(DC):
                    last = t.matmul(ps[bank][:, :], cws(l)[:, kc * P:(kc + 1) * P],
                                    sB(kc * NT + s * TT, TT),
                                    start=(kc == 0), stop=(kc == DC - 1))
                return last
            MMl = pr.add("tensor", t_mm, deps=mmdeps, label=f"cM{idx}_{s}")

            def t_ev(sx, s=s, bank=bank):
                return sx.activation(cev[:, s * TT:(s + 1) * TT], ps[bank][:, :], AF.Identity)
            EV = pr.add("scalar", t_ev, deps=[MMl, cev_rd], label=f"cE{idx}_{s}")
            ev[("cps", bank)] = EV
            EVl.append(EV)
        cw_rd[l] = MMl
        SO = pr.dma("sync", dst, cev, deps=[EVl[-1]], label=f"cS{idx}")
        cev_rd = SO
        lastSO = SO

    WVL = pr.dma("gpsimd", scrB[:, 32768:40960].rearrange("p (kc m) -> p kc m", m=512),
                 wv.rearrange("(kc p) m -> p kc m", p=P),
                 deps=[cw_rd[0], cw_rd[1]], label="cWV")
    lastSV = None
    prevVE = None
    for tsub in range(DC):
        bank = tsub % 2
        half = tsub % 2

        def t_mm(t, tsub=tsub, bank=bank):
            for kc in range(DC):
                last = t.matmul(ps[bank][:, :],
                                scrB[:, kc * NT + tsub * P: kc * NT + (tsub + 1) * P],
                                scrB[:, 32768 + kc * 512:32768 + (kc + 1) * 512],
                                start=(kc == 0), stop=(kc == DC - 1))
            return last
        MM = pr.add("tensor", t_mm, deps=[WVL, ev.get(("vps", bank)), cev_rd if tsub < 2 else None],
                    label=f"cV{tsub}")

        def t_ev(sx, half=half, bank=bank):
            return sx.activation(cev[:, half * 512:(half + 1) * 512], ps[bank][:, :], AF.Identity)
        EV = pr.add("scalar", t_ev, deps=[MM, prevVE], label=f"cVe{tsub}")
        ev[("vps", bank)] = EV

        SV = pr.dma("sync", v1[tsub * P:(tsub + 1) * P, :],
                    cev[:, half * 512:(half + 1) * 512], deps=[EV], label=f"cSV{tsub}")
        prevVE = SV
        lastSV = SV

    def t_fence(v):
        return v.nop()
    pr.add("vector", t_fence, deps=[lastSV], label="fence")
    return pr


def host_inputs_p2(I, p1res, core):
    kp = kperm_local()
    sg1 = wht_perm_sigma1()
    sg2 = wht_perm_sigma2()
    partner = core ^ 1

    def unperm_apart(ap):
        a = np.zeros(2048, np.float32)
        a[kp] = ap[:, 0:DC].T.reshape(-1)
        return a.reshape(DC, P).T

    apc = np.stack([unperm_apart(p1res[f"apart_{core}"]),
                    unperm_apart(p1res[f"apart_{partner}"])], axis=0)
    chv = lambda x, n: f32c(np.asarray(x).reshape(n, P).T)
    H128 = hadamard(128)
    return {
        "s01": p1res[f"s01_{core}"],
        "xn0bf": p1res[f"xn0bf_{core}"],
        "qbf": p1res[f"qbf_{core}"],
        "apc": f32c(apc),
        "such": chv(I["L0_ffn_SU_ug"], DC),
        "sudp": chv(I["L0_ffn_SU_dp"], FC),
        "scup": chv(I["L0_ffn_Wscale_ug"][:8192], FC),
        "scgt": chv(I["L0_ffn_Wscale_ug"][8192:], FC),
        "scdp": chv(I["L0_ffn_Wscale_dp"], DC),
        "h128": bf16(H128),
        "hsb1": bf16(block_diag_h(8, 16)),
        "hsb2": bf16(block_diag_h(2, 64)),
        "wupT": f32c(I["L0_ffn_W_up"].T[sg1, :]),
        "wgtT": f32c(I["L0_ffn_W_gate"].T[sg1, :]),
        "wdnT": f32c(I["L0_ffn_W_down"].T[sg2, :]),
        "wout": f32c(I["L0_attn_W_out"]),
        "bout": chv(I["L0_attn_b_out"], DC),
        "wq": f32c(I["L1_attn_Wq"]),
        "wk": f32c(I["L1_attn_Wk"]),
        "wv": f32c(I["L1_attn_Wv"]),
    }



"""Launch 3 (P4): L1 GQA causal attention + Wo + b_o; LoRA-MoE ffn (base +
routed rank-16 delta); out = x1 + attn1 + ffn1.

Attention per (jb of 4 q-tiles, kv-group g, head): scores computed TRANSPOSED
(sT [keys, q] = k-sub-stationary x q-moving), stabilized by a scalar bound
C = sqrt(max|q|^2 * max|k|^2 / 128) per (g, head, jb): pT = exp(sT - C);
ctx = sum_k v-sub^T pT accumulated in PSUM; L = ones^T pT; the per-q softmax
scale 1/L is applied at ctx evacuation (commutes past the k-sum). Causal
structure: key-subs s <= 2*j+1 with build-time column ranges; the two
boundary subs get per-core mask tiles (data-driven h-parity)."""

P = 128
NT = 2048
TT = 512
D = 2048
DC = 16
NJB = 4
AX = mybir.AxisListType.X
NEG = -1.0e5
RS128 = float(1.0 / np.sqrt(np.float64(128.0)))


def build_p4(nc, ctx):
    x1 = nc.dram_tensor("x1", [D, NT], F32, kind="ExternalInput").ap()
    xn1 = nc.dram_tensor("xn1", [D, NT], BF16, kind="ExternalInput").ap()
    q1 = nc.dram_tensor("q1", [D, NT], BF16, kind="ExternalInput").ap()
    kg = nc.dram_tensor("kg", [4, P, 4096], BF16, kind="ExternalInput").ap()
    vg = nc.dram_tensor("vg", [4096, 512], BF16, kind="ExternalInput").ap()
    mk = nc.dram_tensor("mk", [2, P, P], F32, kind="ExternalInput").ap()
    wo = nc.dram_tensor("wo", [D, D], F32, kind="ExternalInput").ap()
    boch = nc.dram_tensor("boch", [P, DC], F32, kind="ExternalInput").ap()
    wbT = nc.dram_tensor("wbT", [D, D], F32, kind="ExternalInput").ap()
    bbch = nc.dram_tensor("bbch", [P, DC], F32, kind="ExternalInput").ap()
    wrT = nc.dram_tensor("wrT", [D, 8], F32, kind="ExternalInput").ap()
    waT = nc.dram_tensor("waT", [D, P], F32, kind="ExternalInput").ap()
    wbwT = nc.dram_tensor("wbwT", [P, D], F32, kind="ExternalInput").ap()
    e16_d = nc.dram_tensor("e16", [8, P], BF16, kind="ExternalInput").ap()
    i128_d = nc.dram_tensor("i128", [P, P], BF16, kind="ExternalInput").ap()
    out = nc.dram_tensor("out", [D, NT], F32, kind="ExternalOutput").ap()

    scrB = ctx.enter_context(nc.sbuf_tensor("scrB", [P, 45056], BF16))
    scrF = ctx.enter_context(nc.sbuf_tensor("scrF", [P, 18432], F32))
    rows = ctx.enter_context(nc.sbuf_tensor("rows", [1, 6144], F32))
    onesb = ctx.enter_context(nc.sbuf_tensor("onesb", [P, 1], BF16))
    onesf = ctx.enter_context(nc.sbuf_tensor("onesf", [1, P], F32))
    e16 = ctx.enter_context(nc.sbuf_tensor("e16t", [8, P], BF16))
    i128 = ctx.enter_context(nc.sbuf_tensor("i128t", [P, P], BF16))
    wrb = ctx.enter_context(nc.sbuf_tensor("wrb", [P, DC * 8], BF16))
    wab = ctx.enter_context(nc.sbuf_tensor("wab", [P, DC * P], BF16))
    wbwb = ctx.enter_context(nc.sbuf_tensor("wbwb", [P, D], BF16))
    mkt = ctx.enter_context(nc.sbuf_tensor("mkt", [P, 2 * P], F32))
    bos = ctx.enter_context(nc.sbuf_tensor("bos", [P, 2 * DC], F32))

    def sB(off, n):
        return scrB[:, off:off + n]

    def sF(off, n):
        return scrF[:, off:off + n]

    def row(off, n=TT):
        return rows[0:1, off:off + n]

    QJB, CTXB, XNJB, KGT, VLD, PTB, KSQ, QSQ, WSTR, OHB, MSKB = (
        0, 8192, 16384, 24576, 28672, 32768, 33792, 37888, 38400, 42496, 43008)
    X1JB, OUTB, T1S, RLB, LGS = 0, 8192, 16384, 16896, 17408

    ps = [ctx.enter_context(nc.psum_tensor(f"ps{i}", [P, 512], F32)) for i in range(8)]

    pr = Prog(nc)
    ev = {}

    def t_init(v):
        v.memset(onesb[:, :], 1.0)
        return v.memset(onesf[:, :], 1.0)
    INIT = pr.add("vector", t_init, label="init")
    CL = [pr.dma("sync", e16[:, :], e16_d, label="e16", ch=6),
          pr.dma("sync", i128[:, :], i128_d, label="i128", ch=6),
          pr.dma("sync", mkt[:, 0:P], mk[0], label="mk0", ch=6),
          pr.dma("sync", mkt[:, P:2 * P], mk[1], label="mk1", ch=6),
          pr.dma("sync", bos[:, 0:DC], boch, label="bo", ch=6),
          pr.dma("sync", bos[:, DC:2 * DC], bbch, label="bb", ch=6)]
    CG = [pr.dma("gpsimd", wrb[:, :].rearrange("p (c m) -> p c m", m=8),
                 wrT.rearrange("(c p) m -> p c m", p=P), label="wrT", ch=2),
          pr.dma("gpsimd", wab[:, :].rearrange("p (c m) -> p c m", m=P),
                 waT.rearrange("(c p) m -> p c m", p=P), label="waT", ch=2),
          pr.dma("gpsimd", wbwb[:, :], wbwT, label="wbwT", ch=2)]

    prevOUT = None
    prevCTX = {}
    prevXN = None
    prevX1L = None
    prevQ = None
    for jb in range(NJB):
        j0 = 4 * jb
        nsub = 2 * j0 + 8
        qsl = slice(jb * TT, (jb + 1) * TT)

        LQ = pr.dma("sync", sB(QJB, 8192).rearrange("p (c t) -> p c t", t=TT),
                    q1[:, qsl.start:qsl.stop].rearrange("(c p) t -> p c t", p=P),
                    deps=[prevQ], label=f"LQ{jb}", ch=0)
        LXN = pr.dma("sync", sB(XNJB, 8192).rearrange("p (c t) -> p c t", t=TT),
                     xn1[:, qsl.start:qsl.stop].rearrange("(c p) t -> p c t", p=P),
                     deps=[prevXN], label=f"LXN{jb}", ch=3)
        LX1 = pr.dma("sync", sF(X1JB, 8192).rearrange("p (c t) -> p c t", t=TT),
                     x1[:, qsl.start:qsl.stop].rearrange("(c p) t -> p c t", p=P),
                     deps=[prevX1L], label=f"LX1{jb}", ch=4)

        for g in range(4):
            LK = pr.dma("sync", sB(KGT, nsub * P), kg[g, :, 0:nsub * P],
                        deps=[ev.get("kgt")], label=f"LK{jb}_{g}", ch=1)
            LV = pr.dma("sync", sB(VLD, nsub * P).rearrange("p (s m) -> p s m", m=P),
                        vg[0:nsub * P, g * P:(g + 1) * P].rearrange("(s p) m -> p s m", p=P),
                        deps=[ev.get("vld")], label=f"LV{jb}_{g}", ch=2)

            def t_ksq(sx, n=nsub * P):
                return sx.activation(sB(KSQ, n), sB(KGT, n), AF.Square)
            KSQT = pr.add("scalar", t_ksq, deps=[LK, ev.get("ksq")], label=f"ksq{jb}_{g}")

            nch = (nsub * P) // TT
            KROW = []
            for ci in range(nch):
                def t_mm(t, ci=ci):
                    return t.matmul(ps[4][0:1, :], onesb[:, :], sB(KSQ + ci * TT, TT),
                                    start=True, stop=True)
                MM = pr.add("tensor", t_mm, deps=[KSQT, INIT, ev.get("ps4")], label=f"kmm{jb}_{g}_{ci}")

                def t_ev(sx, ci=ci):
                    return sx.activation(row(1536 + ci * TT, TT), ps[4][0:1, :], AF.Identity)
                EVr = pr.add("scalar", t_ev, deps=[MM], label=f"kev{jb}_{g}_{ci}")
                ev["ps4"] = EVr
                KROW.append(EVr)

            def t_kmax(v, n=nsub * P):
                v.reduce_max(rows[0:1, 5632:5633], row(1536, n), axis=AX)
                return v.drain()
            KMX = pr.add("vector", t_kmax, deps=KROW, label=f"kmax{jb}_{g}")

            for hh in range(4):
                head = 4 * g + hh
                qh = sB(QJB + head * TT, TT)

                def t_qsq(sx, qh=qh):
                    return sx.activation(sB(QSQ, TT), qh, AF.Square)
                QSQT = pr.add("scalar", t_qsq, deps=[LQ, ev.get("qsq")], label=f"qsq{jb}_{head}")

                def t_qmm(t):
                    return t.matmul(ps[4][0:1, :], onesb[:, :], sB(QSQ, TT),
                                    start=True, stop=True)
                QMM = pr.add("tensor", t_qmm, deps=[QSQT, ev.get("ps4")], label=f"qmm{jb}_{head}")

                def t_qev(sx):
                    return sx.activation(row(0, TT), ps[4][0:1, :], AF.Identity)
                QEV = pr.add("scalar", t_qev, deps=[QMM, ev.get("qrow")], label=f"qev{jb}_{head}")
                ev["ps4"] = QEV
                ev["qsq"] = QMM

                def t_cb(v):
                    v.reduce_max(rows[0:1, 5633:5634], row(0, TT), axis=AX)
                    v.drain()
                    v.tensor_mul(rows[0:1, 5634:5635], rows[0:1, 5632:5633],
                                 rows[0:1, 5633:5634])
                    return v.drain()
                CB1 = pr.add("vector", t_cb, deps=[QEV, KMX], label=f"cb1_{jb}_{head}")

                def t_cb2(sx):
                    return sx.activation(rows[0:1, 5635:5636], rows[0:1, 5634:5635],
                                         AF.Sqrt, scale=1.0 / 128.0)
                CB2 = pr.add("scalar", t_cb2, deps=[CB1], label=f"cb2_{jb}_{head}")

                def t_cb3(v):
                    return v.tensor_scalar(rows[0:1, 5636:5637], rows[0:1, 5635:5636],
                                           -1.0, None, ALU.mult)
                CB3 = pr.add("vector", t_cb3, deps=[CB2], label=f"cb3_{jb}_{head}")

                def t_cbm(t):
                    return t.matmul(ps[4][:, 0:1], onesf[0:1, :], rows[0:1, 5636:5637],
                                    start=True, stop=True)
                CBM = pr.add("tensor", t_cbm, deps=[CB3, ev.get("ps4")], label=f"cbm{jb}_{head}")

                def t_cbe(sx):
                    return sx.activation(scrF[:, 17440:17441], ps[4][:, 0:1], AF.Identity)
                CBE = pr.add("scalar", t_cbe, deps=[CBM], label=f"cbe{jb}_{head}")
                ev["ps4"] = CBE
                nC = scrF[:, 17440:17441]

                lastCTXMM = None
                lastLMM = None
                for s in range(nsub):
                    o = s - 2 * j0
                    sig = 0 if o < 0 else o // 2
                    coff = sig * P
                    w = TT - coff
                    bank = s % 2

                    def t_st(t, s=s, coff=coff, w=w, bank=bank, head=head):
                        return t.matmul(ps[bank][:, coff:coff + w],
                                        sB(KGT + s * P, P),
                                        sB(QJB + head * TT + coff, w),
                                        start=True, stop=True)
                    ST = pr.add("tensor", t_st, deps=[LK, LQ, ev.get(("pts", bank)), CBE],
                                label=f"st{jb}_{head}_{s}")

                    MSK = None
                    if o >= 0:
                        def t_msk(v, o=o, coff=coff, bank=bank):
                            return v.tensor_add(ps[bank][:, coff:coff + P],
                                                ps[bank][:, coff:coff + P],
                                                mkt[:, (o % 2) * P:(o % 2 + 1) * P])
                        MSK = pr.add("vector", t_msk, deps=[ST, CL[2], CL[3]],
                                     label=f"msk{jb}_{head}_{s}")

                    def t_exp(sx, s=s, coff=coff, w=w, bank=bank):
                        return sx.activation(sB(PTB + bank * TT + coff, w),
                                             ps[bank][:, coff:coff + w], AF.Exp,
                                             scale=RS128, bias=nC[:, 0:1])
                    EXP = pr.add("scalar", t_exp, deps=[MSK if MSK else ST, CBE,
                                                        ev.get(("ptb", bank))],
                                 label=f"exp{jb}_{head}_{s}")

                    def t_ctx(t, s=s, coff=coff, w=w, bank=bank):
                        return t.matmul(ps[2][:, coff:coff + w], sB(VLD + s * P, P),
                                        sB(PTB + bank * TT + coff, w),
                                        start=(s == 0), stop=(s == nsub - 1))
                    CTX = pr.add("tensor", t_ctx,
                                 deps=[EXP, LV, prevCTX.get("ctx") if s == 0 else None],
                                 label=f"ctx{jb}_{head}_{s}")

                    def t_l(t, s=s, coff=coff, w=w, bank=bank):
                        return t.matmul(ps[3][0:1, coff:coff + w], onesb[:, :],
                                        sB(PTB + bank * TT + coff, w),
                                        start=(s == 0), stop=(s == nsub - 1))
                    LMM = pr.add("tensor", t_l, deps=[EXP, prevCTX.get("l") if s == 0 else None],
                                 label=f"l{jb}_{head}_{s}")
                    ev[("pts", bank)] = EXP
                    ev[("ptb", bank)] = LMM
                    lastCTXMM = CTX
                    lastLMM = LMM

                def t_lev(sx):
                    return sx.activation(row(0, TT), ps[3][0:1, :], AF.Identity)
                LEV = pr.add("scalar", t_lev, deps=[lastLMM], label=f"lev{jb}_{head}")
                prevCTX["l"] = LEV

                def t_lrc(v):
                    v.reciprocal(row(TT, TT), row(0, TT))
                    return v.drain()
                LRC = pr.add("vector", t_lrc, deps=[LEV], label=f"lrc{jb}_{head}")
                ev["qrow"] = LRC

                def t_lbc(t):
                    return t.matmul(ps[4][:, :], onesf[0:1, :], row(TT, TT),
                                    start=True, stop=True)
                LBC = pr.add("tensor", t_lbc, deps=[LRC, ev.get("ps4")], label=f"lbc{jb}_{head}")

                def t_lbe(sx):
                    return sx.activation(sF(RLB, TT), ps[4][:, :], AF.Identity)
                LBE = pr.add("scalar", t_lbe, deps=[LBC, ev.get("rlb")], label=f"lbe{jb}_{head}")
                ev["ps4"] = LBE

                def t_cev(v, head=head):
                    return v.tensor_mul(sB(CTXB + head * TT, TT), ps[2][:, :], sF(RLB, TT))
                CEV = pr.add("vector", t_cev, deps=[LBE, lastCTXMM, ev.get("ctxball")],
                             label=f"cev{jb}_{head}")
                prevCTX["ctx"] = CEV
                ev["rlb"] = CEV
            ev["kgt"] = lastCTXMM
            ev["vld"] = lastCTXMM
            ev["ksq"] = KROW[-1]

        def t_lg(t):
            for tsub in range(4):
                for kc in range(DC):
                    last = t.matmul(ps[2][0:P, tsub * 8:(tsub + 1) * 8],
                                    sB(XNJB + kc * TT + tsub * P, P),
                                    wrb[:, kc * 8:(kc + 1) * 8],
                                    start=(kc == 0), stop=(kc == DC - 1))
            return last
        LG = pr.add("tensor", t_lg, deps=[LXN, CG[0], prevCTX.get("ctx")], label=f"lg{jb}")

        def t_lgev(sx):
            return sx.activation(sF(LGS, 32), ps[2][:, 0:32], AF.Identity)
        LGE = pr.add("scalar", t_lgev, deps=[LG], label=f"lgev{jb}")

        def t_oh(v):
            for tsub in range(4):
                v.reduce_max(scrF[:, 17444 + tsub:17445 + tsub],
                             sF(LGS + tsub * 8, 8), axis=AX)
            v.drain()
            for tsub in range(4):
                last = v.tensor_scalar(sB(OHB + tsub * 8, 8), sF(LGS + tsub * 8, 8),
                                       scrF[:, 17444 + tsub:17445 + tsub], None,
                                       ALU.is_equal)
            return last
        OH = pr.add("vector", t_oh, deps=[LGE], label=f"oh{jb}")

        MSKS = []
        for tsub in range(4):
            def t_tr(t, tsub=tsub):
                return t.matmul(ps[3][0:8, 0:P], sB(OHB + tsub * 8, 8), i128[:, :],
                                start=True, stop=True)
            TR = pr.add("tensor", t_tr, deps=[OH, CL[1], ev.get("ps3t"), prevCTX.get("l")],
                        label=f"tr{jb}_{tsub}")

            def t_tre(sx):
                return sx.activation(sB(OHB + 64, P)[0:8, :], ps[3][0:8, 0:P], AF.Identity)
            TRE = pr.add("scalar", t_tre, deps=[TR], label=f"tre{jb}_{tsub}")

            def t_ex(t, tsub=tsub):
                return t.matmul(ps[4][:, 0:P], e16[:, :], sB(OHB + 64, P)[0:8, :],
                                start=True, stop=True)
            EX = pr.add("tensor", t_ex, deps=[TRE, CL[0], ev.get("ps4")], label=f"ex{jb}_{tsub}")

            def t_exe(sx, tsub=tsub):
                return sx.activation(sB(MSKB + tsub * P, P), ps[4][:, 0:P], AF.Identity)
            EXE = pr.add("scalar", t_exe, deps=[EX, ev.get(("mskb", tsub))], label=f"exe{jb}_{tsub}")
            ev["ps4"] = EXE
            ev["ps3t"] = EXE
            ev[("mskb", tsub)] = EXE
            MSKS.append(EXE)

        def t_hl(t):
            for kc in range(DC):
                last = t.matmul(ps[5][:, :], wab[:, kc * P:(kc + 1) * P],
                                sB(XNJB + kc * TT, TT), start=(kc == 0), stop=(kc == DC - 1))
            return last
        HL = pr.add("tensor", t_hl, deps=[LXN, CG[1], ev.get("ps5")], label=f"hl{jb}")

        def t_hg(v):
            return v.scalar_tensor_tensor(sB(43520, TT), ps[5][:, :], 2.0,
                                          sB(MSKB, TT), ALU.mult, ALU.mult)
        HG = pr.add("vector", t_hg, deps=[HL] + MSKS + [ev.get("hgbuf")], label=f"hg{jb}")
        ev["ps5"] = HG

        wstr = lambda l: sB(WSTR + l * 2048, 2048)
        OUTS = []
        for oc in range(DC):
            l = oc % 2
            WOL = pr.dma("gpsimd", wstr(l).rearrange("p (kc m) -> p kc m", m=P),
                         wo[:, oc * P:(oc + 1) * P].rearrange("(kc p) m -> p kc m", p=P),
                         deps=[ev.get(("wstr", l))], label=f"WO{jb}_{oc}", ch=0)

            def t_y(t, oc=oc, l=l):
                for kc in range(DC):
                    last = t.matmul(ps[0][:, :], wstr(l)[:, kc * P:(kc + 1) * P],
                                    sB(CTXB + kc * TT, TT), start=(kc == 0), stop=(kc == DC - 1))
                return last
            ydeps = [WOL, ev.get("ps0y")]
            if oc == 0:
                ydeps.append(prevCTX.get("ctx"))
            MY = pr.add("tensor", t_y, deps=ydeps, label=f"y{jb}_{oc}")

            def t_ye(sx, oc=oc):
                return sx.activation(sF(T1S, TT), ps[0][:, :], AF.Identity,
                                     bias=bos[:, oc:oc + 1])
            YE = pr.add("scalar", t_ye, deps=[MY, ev.get("t1s")], label=f"ye{jb}_{oc}")
            ev["ps0y"] = YE

            WBL = pr.dma("gpsimd", wstr(l).rearrange("p (kc m) -> p kc m", m=P),
                         wbT[:, oc * P:(oc + 1) * P].rearrange("(kc p) m -> p kc m", p=P),
                         deps=[MY], label=f"WB{jb}_{oc}", ch=1)

            def t_b(t, oc=oc, l=l):
                for kc in range(DC):
                    t.matmul(ps[1][:, :], wstr(l)[:, kc * P:(kc + 1) * P],
                             sB(XNJB + kc * TT, TT), start=(kc == 0), stop=False)
                return t.matmul(ps[1][:, :], wbwb[:, oc * P:(oc + 1) * P],
                                sB(43520, TT), start=False, stop=True)
            MB = pr.add("tensor", t_b, deps=[WBL, HG, CG[2], ev.get("ps1b")], label=f"b{jb}_{oc}")

            def t_as(v, oc=oc):
                v.scalar_tensor_tensor(sF(T1S, TT), ps[1][:, :], bos[:, DC + oc:DC + oc + 1],
                                       sF(T1S, TT), ALU.add, ALU.add)
                return v.tensor_add(sF(OUTB + oc * TT, TT), sF(X1JB + oc * TT, TT),
                                    sF(T1S, TT))
            AS = pr.add("vector", t_as, deps=[YE, MB, LX1, ev.get(("outb", oc))],
                        label=f"as{jb}_{oc}")
            ev["ps1b"] = AS
            ev["t1s"] = AS
            ev[("wstr", l)] = MB
            OUTS.append(AS)
            if oc == DC - 1:
                ev["ctxball"] = MY

        SO = pr.dma("sync", out[:, qsl.start:qsl.stop].rearrange("(c p) t -> p c t", p=P),
                    sF(OUTB, 8192).rearrange("p (c t) -> p c t", t=TT),
                    deps=OUTS, label=f"SO{jb}", ch=5)
        for oc in range(DC):
            ev[("outb", oc)] = SO
        prevOUT = SO
        prevXN = MB
        prevX1L = OUTS[-1]
        prevQ = lastCTXMM
        ev["hgbuf"] = MB

    def t_fence(v):
        return v.nop()
    pr.add("vector", t_fence, deps=[prevOUT], label="fence")
    return pr


def host_inputs_p4(I, p2res, core):
    h = core & 1
    partner = core ^ 1

    def interleave_fm(own, other):
        R = own.shape[0]
        out = np.empty((R, 4096), own.dtype)
        for j in range(16):
            out[:, (2 * j + h) * P:(2 * j + h + 1) * P] = own[:, j * P:(j + 1) * P]
            out[:, (2 * j + (1 - h)) * P:(2 * j + (1 - h) + 1) * P] = other[:, j * P:(j + 1) * P]
        return out

    def interleave_tm(own, other):
        C = own.shape[1]
        out = np.empty((4096, C), own.dtype)
        for j in range(16):
            out[(2 * j + h) * P:(2 * j + h + 1) * P] = own[j * P:(j + 1) * P]
            out[(2 * j + (1 - h)) * P:(2 * j + (1 - h) + 1) * P] = other[j * P:(j + 1) * P]
        return out

    kgf = interleave_fm(p2res[f"k1_{core}"], p2res[f"k1_{partner}"]).reshape(4, P, 4096)
    vgf = interleave_tm(p2res[f"v1_{core}"], p2res[f"v1_{partner}"])
    tri = np.where(np.arange(P)[:, None] <= np.arange(P)[None, :], 0.0, NEG).astype(np.float32)
    if h == 0:
        mk0, mk1 = tri, np.full((P, P), NEG, np.float32)
    else:
        mk0, mk1 = np.zeros((P, P), np.float32), tri
    chv = lambda x, n: f32c(np.asarray(x).reshape(n, P).T)
    return {
        "x1": p2res[f"x1_{core}"],
        "xn1": p2res[f"xn1bf_{core}"],
        "q1": p2res[f"q1_{core}"],
        "kg": np.ascontiguousarray(kgf),
        "vg": np.ascontiguousarray(vgf),
        "mk": np.stack([mk0, mk1]),
        "wo": f32c(I["L1_attn_Wo"]),
        "boch": chv(I["L1_attn_bo"], DC),
        "wbT": f32c(I["L1_ffn_W_base"].T),
        "bbch": chv(I["L1_ffn_b_base"], DC),
        "wrT": f32c(I["L1_ffn_router_W"].T),
        "waT": f32c(I["L1_ffn_A"].reshape(P, D).T),
        "wbwT": f32c(I["L1_ffn_Bw"].transpose(0, 2, 1).reshape(P, D)),
        "e16": bf16(np.repeat(np.eye(8, dtype=np.float32), 16, axis=1)),
        "i128": bf16(np.eye(P, dtype=np.float32)),
    }



_CACHE = {}


def _programs():
    if "nc" in _CACHE:
        return _CACHE["nc"]
    ncs = []
    for build in (build_p1, build_p2, build_p4):
        nc = bass.Bass("TRN2", target_bir_lowering=False, debug=False)
        ctx = contextlib.ExitStack()
        pr = build(nc, ctx)
        pr.build(ctx)
        ctx.close()
        ncs.append(nc)
    _CACHE["nc"] = ncs
    return ncs


def kernel(L0_attn_X, L0_attn_W_in, L0_attn_b_in, L0_attn_W_out, L0_attn_b_out,
           L0_attn_gamma, L0_attn_beta, L0_ffn_SU_ug, L0_ffn_SU_dp,
           L0_ffn_Wscale_ug, L0_ffn_Wscale_dp, L0_ffn_W_up, L0_ffn_W_gate,
           L0_ffn_W_down, L1_attn_Wq, L1_attn_Wk, L1_attn_Wv, L1_attn_Wo,
           L1_attn_bo, L1_ffn_W_base, L1_ffn_b_base, L1_ffn_router_W, L1_ffn_A,
           L1_ffn_Bw):
    I = {k: np.asarray(v) for k, v in locals().items()}
    nc1, nc2, nc3 = _programs()

    in1 = [host_inputs_p1(I["L0_attn_X"], I["L0_attn_W_in"], I["L0_attn_b_in"],
                          I["L0_attn_gamma"], I["L0_attn_beta"], c)
           for c in range(8)]
    r1 = run_bass_kernel_spmd(nc1, in1, core_ids=list(range(8)))
    p1res = {f"{k}_{c}": r1.results[c][k] for c in range(8)
             for k in ("s01", "xn0bf", "qbf", "apart")}

    in2 = [host_inputs_p2(I, p1res, c) for c in range(8)]
    r2 = run_bass_kernel_spmd(nc2, in2, core_ids=list(range(8)))
    p2res = {f"{k}_{c}": r2.results[c][k] for c in range(8)
             for k in ("x1", "xn1bf", "q1", "k1", "v1")}

    in3 = [host_inputs_p4(I, p2res, c) for c in range(8)]
    r3 = run_bass_kernel_spmd(nc3, in3, core_ids=list(range(8)))

    out = np.empty((4, 4096, 2048), np.float32)
    for c in range(8):
        b = c >> 1
        rows_idx = core_positions(c)
        out[b, rows_idx, :] = r3.results[c]["out"].T
    return out
